# version 04
# baseline (speedup 1.0000x reference)
import numpy as np
import ml_dtypes
from contextlib import ExitStack

import concourse.bass as bass
import concourse.mybir as mybir
from concourse.bass_utils import run_bass_kernel_spmd

F32 = mybir.dt.float32
BF16 = mybir.dt.bfloat16
I32 = mybir.dt.int32
AF = mybir.ActivationFunctionType
ALU = mybir.AluOpType
AXX = mybir.AxisListType.X
BF16NP = ml_dtypes.bfloat16

S, D, NH, DK, II = 8192, 2048, 16, 128, 8192
NCORE = 8
TC = S // NCORE
NT = TC // 128
DCH = D // 128
ICH = II // 128
ISQRT_DK = float(1.0 / np.sqrt(128.0))
C1 = float(1.0 / np.sqrt(2048.0))
C2 = float(1.0 / np.sqrt(8192.0))
WMEAN_INV = float(1.0 / (8192.0 * 2048.0))
ENGS = ("sync", "tensor", "vector", "scalar", "gpsimd")
DEBUG_L3 = False


def _hadamard(n):
    m = np.array([[1.0]], dtype=np.float64)
    while m.shape[0] < n:
        m = np.block([[m, m], [m, -m]])
    return m


class Sched:

    def __init__(self):
        self.items = {e: [] for e in ENGS}
        self.counts = {}
        self.guard = {}

    def op(self, eng, build, inc=None, inc_n=1, waits=()):
        self.items[eng].append((build, inc, inc_n, [(s, c) for (s, c) in waits if c > 0]))
        if inc is not None:
            k = id(inc)
            self.counts[k] = self.counts.get(k, 0) + inc_n

    def now(self, sem):
        return self.counts.get(id(sem), 0)

    def dma(self, eng, out, in_, inc, waits=()):
        self.op(eng, lambda e, o=out, i=in_: e.dma_start(out=o, in_=i),
                inc=inc, inc_n=16, waits=waits)

    def gwait(self, key):
        g = self.guard.get(key)
        return [g] if g else []

    def gset(self, key, sem):
        self.guard[key] = (sem, self.now(sem))

    def barrier(self, engines, sems, stream_sems):
        snap = [(s, self.now(s)) for s in sems]
        for e in engines:
            inc = stream_sems.get(e)
            self.op(e, lambda _e: None, inc=inc, inc_n=1 if inc is not None else 1,
                    waits=snap)

    def replay(self, nc, serial=()):
        cm = nc.Block()
        block = cm.__enter__()
        for eng_name in ENGS:
            items = self.items[eng_name]
            if not items:
                continue

            def body(engine, items=items, eng_name=eng_name):
                seen = {}
                ssem = serial.get(eng_name) if isinstance(serial, dict) else None
                idx = 0
                for build, inc, inc_n, waits in items:
                    if ssem is not None and idx > 0:
                        engine.wait_ge(ssem, idx)
                        seen[id(ssem)] = max(seen.get(id(ssem), 0), idx)
                    for sm, cnt in waits:
                        k = id(sm)
                        if seen.get(k, 0) < cnt:
                            engine.wait_ge(sm, cnt)
                            seen[k] = cnt
                    ins = build(engine)
                    if inc is not None:
                        if ins is None:
                            ins = engine.nop()
                        ins.then_inc(inc, inc_n)
                    if ssem is not None:
                        assert inc is ssem and inc_n == 1, \
                            f"{eng_name} op must inc its stream sem"
                        idx += 1
            getattr(block, eng_name)(body)
        cm.__exit__(None, None, None)


def emit_fold(sh, eng, sem, pair0, fold_t, width, op, dst_final, waits0=()):
    h = width // 2
    a0, b0 = pair0
    if h == 1:
        sh.op(eng, lambda e, a=a0, b=b0: e.tensor_tensor(dst_final, a, b, op),
              inc=sem, waits=list(waits0))
        return
    sh.op(eng, lambda e, a=a0, b=b0, h=h: e.tensor_tensor(fold_t[:, 0:h], a, b, op),
          inc=sem, waits=list(waits0))
    while h > 2:
        nh = h // 2
        sh.op(eng, lambda e, nh=nh, h=h: e.tensor_tensor(
            fold_t[:, 0:nh], fold_t[:, 0:nh], fold_t[:, nh:h], op), inc=sem)
        h = nh
    sh.op(eng, lambda e: e.tensor_tensor(dst_final, fold_t[:, 0:1], fold_t[:, 1:2], op),
          inc=sem)


def _consts(nc):
    return {
        "ident_bf": nc.inline_tensor(np.eye(128, dtype=BF16NP), "ident_bf"),
        "ident_f32": nc.inline_tensor(np.eye(128, dtype=np.float32), "ident_f32"),
        "had_bf": nc.inline_tensor(_hadamard(128).astype(BF16NP), "had_bf"),
        "ones_row": nc.inline_tensor(np.ones((1, 128), dtype=np.float32), "ones_row"),
    }



def build_l1():
    nc = bass.Bass("TRN2", target_bir_lowering=False)
    cst = _consts(nc)
    x_h = nc.dram_tensor("x_own", [TC, D], F32, kind="ExternalInput")
    w_hs = {n: nc.dram_tensor("w" + n, [D, D], F32, kind="ExternalInput") for n in "qkv"}
    b_hs = {n: nc.dram_tensor("b" + n, [128, DCH] if n in "qk" else [1, D], F32,
                              kind="ExternalInput") for n in "qkv"}
    wsl_hs = [
        nc.dram_tensor("wg_sl", [II // 8, D], F32, kind="ExternalInput"),
        nc.dram_tensor("wu_sl", [II // 8, D], F32, kind="ExternalInput"),
        nc.dram_tensor("wd_sl", [D // 8, II], F32, kind="ExternalInput"),
    ]
    xnt_o = nc.dram_tensor("xnt_out", [128, DCH * TC], BF16, kind="ExternalOutput")
    proj_os = {"q": nc.dram_tensor("qt_out", [D, TC], BF16, kind="ExternalOutput"),
               "k": nc.dram_tensor("kt_out", [D, TC], BF16, kind="ExternalOutput")}
    v_o = nc.dram_tensor("v_out", [TC, D], BF16, kind="ExternalOutput")
    ws_o = nc.dram_tensor("ws_out", [128, 4], F32, kind="ExternalOutput")

    with ExitStack() as stk:
        sb = lambda n, s, d: stk.enter_context(nc.sbuf_tensor(n, s, d))
        ps = lambda n, s, d: stk.enter_context(nc.psum_tensor(n, s, d))
        sem = lambda n: stk.enter_context(nc.semaphore(n))

        Xt = sb("Xt", [128, D], F32)
        XnF = sb("XnF", [128, D], F32)
        XnB = sb("XnB", [128, D], BF16)
        SqT = sb("SqT", [128, D], F32)
        FoldA = sb("FoldA", [128, D // 2], F32)
        St = sb("St", [128, 8], F32)
        xnT = sb("xnT", [128, DCH * TC], BF16)
        WT = sb("WT", [128, DCH * D], BF16)
        Wld = sb("Wld", [128, D], F32)
        WldB = sb("WldB", [128, D], BF16)
        BiasQK = sb("BiasQK", [128, 2 * DCH], F32)
        BvRow = sb("BvRow", [1, D], F32)
        Bvb = sb("Bvb", [128, D], F32)
        OnesR = sb("OnesR", [1, 128], F32)
        Proj = sb("Proj", [128, 1024], BF16)
        Vout = sb("Vout", [128, D], BF16)
        WsAcc = sb("WsAcc", [128, 4], F32)
        Wabs = sb("Wabs", [128, II], F32)
        Rcol = sb("Rcol", [128, 1], F32)
        IdB = sb("IdB", [128, 128], BF16)
        Eps = sb("Eps", [128, 1], F32)
        nc.const_aps.register(F32, 1e-5, Eps[:, :])

        PT = [ps("PT0", [128, 128], BF16), ps("PT1", [128, 128], BF16)]
        PP = [ps("PP0", [128, 512], F32), ps("PP1", [128, 512], F32)]
        PV = [ps(f"PV{i}", [128, 512], F32) for i in range(4)]

        ld = sem("ld"); gl = sem("gl"); tv = sem("tv"); tsc = sem("tsc")
        tt = sem("tt"); so = sem("so")
        sh = Sched()

        def transpose_block(src_ap, dst_ap, bank):
            sh.op("tensor", lambda e, s=src_ap, b=bank: e.transpose(b[:, :], s, IdB[:, :]),
                  inc=tt, waits=sh.gwait(("bk", id(bank))) + sh.gwait("tsrc"))
            sh.op("scalar", lambda e, b=bank, d=dst_ap: e.activation(d, b[:, :], AF.Copy),
                  inc=tsc, waits=[(tt, sh.now(tt))])
            sh.gset(("bk", id(bank)), tsc)

        sh.dma("sync", IdB[:, :], cst["ident_bf"][:, :], ld)
        sh.dma("sync", OnesR[:, :], cst["ones_row"][:, :], ld)
        sh.dma("sync", BiasQK[:, 0:DCH], b_hs["q"][:, :], ld)
        sh.dma("sync", BiasQK[:, DCH:2 * DCH], b_hs["k"][:, :], ld)
        sh.dma("sync", BvRow[:, :], b_hs["v"][0:1, :], ld)
        sh.op("vector", lambda e: e.memset(WsAcc[:, :], 0.0), inc=tv)
        sh.op("vector", lambda e: e.memset(Eps[:, :], 1e-5), inc=tv)

        for i in range(NT):
            sh.dma("sync", Xt[:, :], x_h[i * 128:(i + 1) * 128, :], ld,
                   waits=([(tsc, sh.now(tsc))] if i else []))
            lw = [(ld, sh.now(ld))]
            sh.op("scalar", lambda e: e.activation(SqT[:, :], Xt[:, :], AF.Square),
                  inc=tsc, waits=lw)
            emit_fold(sh, "vector", tv, (Xt[:, 0:D // 2], Xt[:, D // 2:D]),
                      FoldA, D, ALU.add, St[:, 0:1], waits0=lw)
            emit_fold(sh, "vector", tv, (SqT[:, 0:D // 2], SqT[:, D // 2:D]),
                      SqT, D, ALU.add, St[:, 1:2], waits0=[(tsc, sh.now(tsc))])
            sh.op("vector", lambda e: e.tensor_scalar_mul(St[:, 2:3], St[:, 0:1], 1.0 / D), inc=tv)
            sh.op("vector", lambda e: e.tensor_scalar_mul(St[:, 3:4], St[:, 1:2], 1.0 / D), inc=tv)
            sh.op("vector", lambda e: e.scalar_tensor_tensor(
                St[:, 4:5], St[:, 2:3], St[:, 2:3], St[:, 3:4], ALU.mult, ALU.subtract), inc=tv)
            sh.op("scalar", lambda e: e.activation(St[:, 5:6], St[:, 4:5], AF.Sqrt,
                                                   scale=-1.0, bias=1e-5),
                  inc=tsc, waits=[(tv, sh.now(tv))])
            sh.op("vector", lambda e: e.reciprocal(St[:, 6:7], St[:, 5:6]),
                  inc=tv, waits=[(tsc, sh.now(tsc))])
            sh.op("vector", lambda e: e.tensor_scalar(
                St[:, 7:8], St[:, 2:3], St[:, 6:7], -1.0, ALU.mult, ALU.mult), inc=tv)
            sh.op("scalar", lambda e: e.activation(XnF[:, :], Xt[:, :], AF.Identity,
                                                   scale=St[:, 6:7], bias=St[:, 7:8]),
                  inc=tsc, waits=[(tv, sh.now(tv))])
            sh.op("vector", lambda e: e.tensor_copy(XnB[:, :], XnF[:, :]),
                  inc=tv, waits=[(tsc, sh.now(tsc))] + sh.gwait("xnbfree"))
            sh.guard["tsrc"] = (tv, sh.now(tv))
            for j in range(DCH):
                transpose_block(XnB[:, j * 128:(j + 1) * 128],
                                xnT[:, j * TC + i * 128: j * TC + (i + 1) * 128],
                                PT[j % 2])
            sh.guard["xnbfree"] = (tt, sh.now(tt))
        sh.dma("gpsimd", xnt_o[:, :], xnT[:, :], so, waits=[(tsc, sh.now(tsc))])

        def build_wt(w_h):
            for i in range(DCH):
                sh.dma("sync", Wld[:, :], w_h[i * 128:(i + 1) * 128, :], ld,
                       waits=sh.gwait("wldfree"))
                sh.op("vector", lambda e: e.tensor_copy(WldB[:, :], Wld[:, :]),
                      inc=tv, waits=[(ld, sh.now(ld))] + sh.gwait("wldbfree"))
                sh.gset("wldfree", tv)
                sh.guard["tsrc"] = (tv, sh.now(tv))
                for j in range(DCH):
                    transpose_block(WldB[:, j * 128:(j + 1) * 128],
                                    WT[:, j * D + i * 128: j * D + (i + 1) * 128],
                                    PT[j % 2])
                sh.guard["wldbfree"] = (tt, sh.now(tt))

        for wi, wname in enumerate(["q", "k"]):
            build_wt(w_hs[wname])
            wtr = (tsc, sh.now(tsc))
            for ci in range(DCH):
                for h in range(2):
                    bank = PP[h]
                    for j in range(DCH):
                        w = [wtr] + (sh.gwait(("bk", id(bank))) if j == 0 else [])
                        sh.op("tensor", lambda e, b=bank, j=j, ci=ci, h=h: e.matmul(
                            b[:, :], WT[:, j * D + ci * 128:j * D + (ci + 1) * 128],
                            xnT[:, j * TC + h * 512:j * TC + (h + 1) * 512],
                            start=(j == 0), stop=(j == DCH - 1)), inc=tt, waits=w)
                    sh.op("scalar", lambda e, b=bank, ci=ci, h=h, wi=wi: e.activation(
                        Proj[:, h * 512:(h + 1) * 512], b[:, :], AF.Identity,
                        bias=BiasQK[:, wi * DCH + ci:wi * DCH + ci + 1], scale=1.0),
                        inc=tsc, waits=[(tt, sh.now(tt))] + sh.gwait("projfree"))
                    sh.gset(("bk", id(bank)), tsc)
                sh.dma("gpsimd", proj_os[wname][ci * 128:(ci + 1) * 128, :], Proj[:, :],
                       so, waits=[(tsc, sh.now(tsc))])
                sh.guard["projfree"] = (so, sh.now(so))

        for b4 in range(4):
            bank = PV[b4]
            sh.op("tensor", lambda e, b=bank, b4=b4: e.matmul(
                b[:, :], OnesR[0:1, :], BvRow[0:1, b4 * 512:(b4 + 1) * 512],
                start=True, stop=True), inc=tt, waits=[(ld, sh.now(ld))])
            sh.op("scalar", lambda e, b=bank, b4=b4: e.activation(
                Bvb[:, b4 * 512:(b4 + 1) * 512], b[:, :], AF.Copy),
                inc=tsc, waits=[(tt, sh.now(tt))])
            sh.gset(("bk", id(bank)), tsc)
        build_wt(w_hs["v"])
        wtr = (tsc, sh.now(tsc))
        for i in range(NT):
            for b4 in range(4):
                bank = PV[b4]
                for j in range(DCH):
                    w = [wtr] + (sh.gwait(("bk", id(bank))) if j == 0 else [])
                    sh.op("tensor", lambda e, b=bank, j=j, i=i, b4=b4: e.matmul(
                        b[:, :], xnT[:, j * TC + i * 128:j * TC + (i + 1) * 128],
                        WT[:, j * D + b4 * 512:j * D + (b4 + 1) * 512],
                        start=(j == 0), stop=(j == DCH - 1)), inc=tt, waits=w)
                sh.op("vector", lambda e, b=bank, b4=b4: e.tensor_tensor(
                    Vout[:, b4 * 512:(b4 + 1) * 512], b[:, :],
                    Bvb[:, b4 * 512:(b4 + 1) * 512], ALU.add),
                    inc=tv, waits=[(tt, sh.now(tt))] + sh.gwait("voutfree"))
                sh.gset(("bk", id(bank)), tv)
            sh.dma("gpsimd", v_o[i * 128:(i + 1) * 128, :], Vout[:, :], so,
                   waits=[(tv, sh.now(tv))])
            sh.guard["voutfree"] = (so, sh.now(so))

        for wi, wh in enumerate(wsl_hs):
            rows, cols = wh.shape
            for i in range(rows // 128):
                sh.dma("gpsimd", Wabs[:, 0:cols], wh[i * 128:(i + 1) * 128, :], gl,
                       waits=sh.gwait("wabsfree"))
                sh.op("vector", lambda e, cols=cols: e.scalar_tensor_tensor(
                    Wabs[:, 0:cols], Wabs[:, 0:cols], -1.0, Wabs[:, 0:cols],
                    ALU.mult, ALU.max), inc=tv, waits=[(gl, sh.now(gl))])
                emit_fold(sh, "vector", tv, (Wabs[:, 0:cols // 2], Wabs[:, cols // 2:cols]),
                          Wabs, cols, ALU.add, Rcol[:, :])
                sh.gset("wabsfree", tv)
                sh.op("vector", lambda e, wi=wi: e.tensor_tensor(
                    WsAcc[:, wi:wi + 1], WsAcc[:, wi:wi + 1], Rcol[:, :], ALU.add), inc=tv)
        sh.dma("gpsimd", ws_o[:, :], WsAcc[:, :], so, waits=[(tv, sh.now(tv))])

        sh.op("sync", lambda e: None,
              waits=[(so, sh.now(so)), (ld, sh.now(ld)), (gl, sh.now(gl)),
                     (tsc, sh.now(tsc)), (tv, sh.now(tv)), (tt, sh.now(tt))])

        sh.replay(nc, serial={"vector": tv, "scalar": tsc})
    return nc


_CACHE = {}


def run_l1(inputs):
    if "l1" not in _CACHE:
        _CACHE["l1"] = build_l1()
    nc = _CACHE["l1"]
    x = np.ascontiguousarray(inputs["L0_attn_x"].reshape(S, D))
    in_maps = []
    for c in range(NCORE):
        in_maps.append({
            "x_own": x[c * TC:(c + 1) * TC, :],
            "wq": inputs["L0_attn_Wq"], "wk": inputs["L0_attn_Wk"], "wv": inputs["L0_attn_Wv"],
            "bq": np.ascontiguousarray(inputs["L0_attn_bq"].reshape(DCH, 128).T),
            "bk": np.ascontiguousarray(inputs["L0_attn_bk"].reshape(DCH, 128).T),
            "bv": inputs["L0_attn_bv"].reshape(1, D),
            "wg_sl": inputs["L0_ffn_w_gate"][c * (II // 8):(c + 1) * (II // 8), :],
            "wu_sl": inputs["L0_ffn_w_up"][c * (II // 8):(c + 1) * (II // 8), :],
            "wd_sl": inputs["L0_ffn_w_down"][c * (D // 8):(c + 1) * (D // 8), :],
        })
    res = run_bass_kernel_spmd(nc, in_maps, core_ids=list(range(NCORE)))
    return res.results





def build_l2():
    nc = bass.Bass("TRN2", target_bir_lowering=False)
    cst = _consts(nc)
    qt_h = nc.dram_tensor("qt_own", [D, 1024], BF16, kind="ExternalInput")
    kt_h = nc.dram_tensor("kt_full", [D, S], BF16, kind="ExternalInput")
    v_h = nc.dram_tensor("v_full", [S, D], BF16, kind="ExternalInput")
    wo_h = nc.dram_tensor("wo", [D, D], F32, kind="ExternalInput")
    bo_h = nc.dram_tensor("bo", [1, D], F32, kind="ExternalInput")
    ws_h = nc.dram_tensor("ws_all", [128, 32], F32, kind="ExternalInput")
    msk_h = nc.dram_tensor("masks", [2, 128, 64 * 512], BF16, kind="ExternalInput")
    wsl_hs = [
        nc.dram_tensor("wg_sl", [II // 8, D], F32, kind="ExternalInput"),
        nc.dram_tensor("wu_sl", [II // 8, D], F32, kind="ExternalInput"),
        nc.dram_tensor("wd_sl", [D // 8, II], F32, kind="ExternalInput"),
    ]
    attn_o = nc.dram_tensor("attn_out", [1024, D], F32, kind="ExternalOutput")
    wt_os = [nc.dram_tensor("wgt_out", [D, II // 8], BF16, kind="ExternalOutput"),
             nc.dram_tensor("wut_out", [D, II // 8], BF16, kind="ExternalOutput"),
             nc.dram_tensor("wdt_out", [II, D // 8], BF16, kind="ExternalOutput")]
    s_o = nc.dram_tensor("s_out", [1, 4], F32, kind="ExternalOutput")

    with ExitStack() as stk:
        sb = lambda n, s_, d: stk.enter_context(nc.sbuf_tensor(n, s_, d))
        ps = lambda n, s_, d: stk.enter_context(nc.psum_tensor(n, s_, d))
        sem = lambda n: stk.enter_context(nc.semaphore(n))

        IdB = sb("IdB", [128, 128], BF16)
        IdF = sb("IdF", [128, 128], F32)
        OnesR = sb("OnesR", [1, 128], F32)
        ctx = sb("ctx", [128, 8 * D], BF16)
        Bob = sb("Bob", [128, D], F32)
        BoRow = sb("BoRow", [1, D], F32)
        Ws = sb("Ws", [128, 32], F32)
        Acc2 = sb("Acc2", [128, 4], F32)
        AccT = sb("AccT", [4, 128], F32)
        Ssum = sb("Ssum", [4, 1], F32)
        Sval = sb("Sval", [4, 1], F32)
        Srow = sb("Srow", [1, 4], F32)
        Sbc = sb("Sbc", [128, 4], F32)
        Sh = sb("Sh", [128, 4], F32)
        Nsh = sb("Nsh", [128, 4], F32)
        Rq = sb("Rq", [128, 1], F32)

        ld = sem("ld"); gl = sem("gl"); tv = sem("tv"); tsc = sem("tsc")
        tt = sem("tt"); so = sem("so")
        ALLSEMS = [ld, gl, tv, tsc, tt, so]
        STREAMS = {"vector": tv, "scalar": tsc}
        sh = Sched()

        def bar():
            sh.barrier(ENGS, ALLSEMS, STREAMS)

        sh.dma("sync", IdB[:, :], cst["ident_bf"][:, :], ld)
        sh.dma("sync", IdF[:, :], cst["ident_f32"][:, :], ld)
        sh.dma("sync", OnesR[:, :], cst["ones_row"][:, :], ld)
        sh.dma("sync", BoRow[:, :], bo_h[:, :], ld)
        sh.dma("sync", Ws[:, :], ws_h[:, :], ld)

        with nc.psum_tensor("PTW", [128, 128], BF16) as PTW, \
             nc.psum_tensor("PTW2", [128, 128], BF16) as PTW2, \
             nc.psum_tensor("PTF", [128, 128], F32) as PTF, \
             nc.psum_tensor("PBC", [128, 512], F32) as PBC, \
             nc.sbuf_tensor("Wld2", [128, II], F32) as Wld2, \
             nc.sbuf_tensor("T1", [128, II], BF16) as T1, \
             nc.sbuf_tensor("T2", [128, II], BF16) as T2, \
             nc.sbuf_tensor("Stage", [128, II], BF16) as Stage:
            lw = [(ld, sh.now(ld))]
            sh.op("vector", lambda e: e.tensor_copy(Acc2[:, :], Ws[:, 0:4]), inc=tv, waits=lw)
            for c in range(1, 8):
                sh.op("vector", lambda e, c=c: e.tensor_tensor(
                    Acc2[:, :], Acc2[:, :], Ws[:, 4 * c:4 * (c + 1)], ALU.add), inc=tv)
            sh.op("tensor", lambda e: e.transpose(PTF[0:4, :], Acc2[:, :], IdF[:, :]),
                  inc=tt, waits=[(tv, sh.now(tv))])
            sh.op("scalar", lambda e: e.activation(AccT[:, :], PTF[0:4, :], AF.Copy),
                  inc=tsc, waits=[(tt, sh.now(tt))])
            emit_fold(sh, "vector", tv, (AccT[:, 0:64], AccT[:, 64:128]),
                      AccT, 128, ALU.add, Ssum[:, :], waits0=[(tsc, sh.now(tsc))])
            sh.op("vector", lambda e: e.tensor_scalar(
                Sval[:, :], Ssum[:, :], WMEAN_INV, 1e-5, ALU.mult, ALU.max), inc=tv)
            sh.dma("gpsimd", Srow[:, :], Sval[:, :], gl, waits=[(tv, sh.now(tv))])
            sh.dma("gpsimd", s_o[:, :], Srow[:, :], so, waits=[(gl, sh.now(gl))])
            sh.op("tensor", lambda e: e.matmul(PBC[:, 0:4], OnesR[0:1, :], Srow[0:1, :],
                                               start=True, stop=True),
                  inc=tt, waits=[(gl, sh.now(gl))])
            sh.op("scalar", lambda e: e.activation(Sbc[:, :], PBC[:, 0:4], AF.Copy),
                  inc=tsc, waits=[(tt, sh.now(tt))])
            sh.op("vector", lambda e: e.tensor_scalar_mul(Sh[:, :], Sbc[:, :], 0.5),
                  inc=tv, waits=[(tsc, sh.now(tsc))])
            sh.op("vector", lambda e: e.tensor_scalar_mul(Nsh[:, :], Sbc[:, :], -0.5), inc=tv)
            svr = sh.now(tv)
            PTs = [PTW, PTW2]
            for wi, wh in enumerate(wsl_hs):
                rows, cols = wh.shape
                nblk = cols // 128
                for i in range(rows // 128):
                    sh.dma("sync", Wld2[:, 0:cols], wh[i * 128:(i + 1) * 128, :], ld,
                           waits=sh.gwait("wld2free"))
                    lw = [(ld, sh.now(ld)), (tv, svr)]
                    sh.op("vector", lambda e, cols=cols, wi=wi: e.tensor_scalar(
                        T1[:, 0:cols], Wld2[:, 0:cols], Sh[:, wi:wi + 1], None, ALU.is_ge),
                        inc=tv, waits=lw + sh.gwait("t1free"))
                    sh.op("vector", lambda e, cols=cols, wi=wi: e.tensor_scalar(
                        T2[:, 0:cols], Wld2[:, 0:cols], Nsh[:, wi:wi + 1], None, ALU.is_le),
                        inc=tv)
                    sh.gset("wld2free", tv)
                    sh.op("vector", lambda e, cols=cols: e.tensor_tensor(
                        T1[:, 0:cols], T1[:, 0:cols], T2[:, 0:cols], ALU.subtract), inc=tv)
                    tern_r = sh.now(tv)
                    for j in range(nblk):
                        bank = PTs[j % 2]
                        sh.op("tensor", lambda e, b=bank, j=j: e.transpose(
                            b[:, :], T1[:, j * 128:(j + 1) * 128], IdB[:, :]),
                            inc=tt, waits=[(tv, tern_r)] + sh.gwait(("bk", id(bank))))
                        sh.op("scalar", lambda e, b=bank, j=j: e.activation(
                            Stage[:, j * 128:(j + 1) * 128], b[:, :], AF.Copy),
                            inc=tsc, waits=[(tt, sh.now(tt))] + sh.gwait("stagefree"))
                        sh.gset(("bk", id(bank)), tsc)
                    sh.guard["t1free"] = (tt, sh.now(tt))
                    sh.dma("gpsimd",
                           wt_os[wi][:, i * 128:(i + 1) * 128].rearrange(
                               "(j p) c -> p j c", p=128),
                           Stage[:, 0:cols].rearrange("p (j c) -> p j c", c=128),
                           so, waits=[(tsc, sh.now(tsc))])
                    sh.guard["stagefree"] = (so, sh.now(so))
            bar()

        with nc.psum_tensor("PBo", [128, 512], F32) as PBo:
            for b4 in range(4):
                sh.op("tensor", lambda e, b4=b4: e.matmul(
                    PBo[:, :], OnesR[0:1, :], BoRow[0:1, b4 * 512:(b4 + 1) * 512],
                    start=True, stop=True), inc=tt,
                    waits=[(ld, sh.now(ld))] + sh.gwait("pbo"))
                sh.op("scalar", lambda e, b4=b4: e.activation(
                    Bob[:, b4 * 512:(b4 + 1) * 512], PBo[:, :], AF.Copy),
                    inc=tsc, waits=[(tt, sh.now(tt))])
                sh.gset("pbo", tsc)
            bar()

        with nc.psum_tensor("PSa", [128, 1024], F32) as PSa, \
             nc.psum_tensor("PSb", [128, 1024], F32) as PSb, \
             nc.psum_tensor("PC0", [128, 512], F32) as PC0, \
             nc.psum_tensor("PC1", [128, 512], F32) as PC1, \
             nc.psum_tensor("PC2", [128, 512], F32) as PC2, \
             nc.psum_tensor("PC3", [128, 512], F32) as PC3, \
             nc.sbuf_tensor("Msk", [128, 64 * 512], BF16) as Msk, \
             nc.sbuf_tensor("KTh", [128, S], BF16) as KTh, \
             nc.sbuf_tensor("Von", [128, 64 * 129], BF16) as Von, \
             nc.sbuf_tensor("QTc", [128, 512], BF16) as QTc, \
             nc.sbuf_tensor("Pt0", [128, 1024], BF16) as Pt0, \
             nc.sbuf_tensor("Pt1", [128, 1024], BF16) as Pt1:
            PCs = [PC0, PC1, PC2, PC3]
            Pts = [Pt0, Pt1]
            PSs = [PSa, PSb]
            sh.op("vector", lambda e: e.memset(
                Von[:, :].rearrange("p (k c) -> p k c", c=129)[:, :, 128:129], 1.0), inc=tv)
            vonones = sh.now(tv)
            for ch in range(2):
                sh.dma("sync", Msk[:, :], msk_h[ch, :, :], ld,
                       waits=sh.gwait("mskfree"))
                mskr = sh.now(ld)
                for head in range(NH):
                    sh.dma("sync", KTh[:, :], kt_h[head * 128:(head + 1) * 128, :], ld,
                           waits=sh.gwait("kthfree"))
                    kthr = sh.now(ld)
                    sh.dma("sync",
                           Von[:, :].rearrange("p (k c) -> p k c", c=129)[:, :, 0:128],
                           v_h[:, head * 128:(head + 1) * 128].rearrange(
                               "(k p) e -> p k e", p=128),
                           ld, waits=sh.gwait("vonfree") + [(tv, vonones)])
                    vonr = sh.now(ld)
                    sh.dma("sync", QTc[:, :],
                           qt_h[head * 128:(head + 1) * 128, ch * 512:(ch + 1) * 512],
                           ld, waits=sh.gwait("qtcfree"))
                    qtr = sh.now(ld)
                    for kp in range(32):
                        psb = PSs[kp % 2]
                        ptb = Pts[kp % 2]
                        for kl in range(2):
                            kt = 2 * kp + kl
                            w = [(ld, max(kthr, qtr))]
                            if kl == 0:
                                w += sh.gwait(("ps", kp % 2))
                            sh.op("tensor", lambda e, b=psb, kt=kt, kl=kl: e.matmul(
                                b[:, kl * 512:(kl + 1) * 512],
                                KTh[:, kt * 128:(kt + 1) * 128], QTc[:, :],
                                start=True, stop=True), inc=tt, waits=w)
                        sh.op("scalar", lambda e, pb=psb, ptb=ptb: e.activation(
                            ptb[:, :], pb[:, :], AF.Exp, scale=ISQRT_DK),
                            inc=tsc, waits=[(tt, sh.now(tt))] + sh.gwait(("pt", kp % 2)))
                        sh.op("vector", lambda e, ptb=ptb, kp=kp: e.tensor_tensor(
                            ptb[:, :], ptb[:, :], Msk[:, kp * 1024:(kp + 1) * 1024],
                            ALU.mult), inc=tv,
                            waits=[(tsc, sh.now(tsc)), (ld, mskr)])
                        sh.gset(("ps", kp % 2), tv)
                        mskd = sh.now(tv)
                        for kl in range(2):
                            kt = 2 * kp + kl
                            for sub in range(4):
                                sh.op("tensor", lambda e, pc=PCs[sub], ptb=ptb, kl=kl,
                                      sub=sub, kt=kt, kp=kp: e.matmul(
                                    pc[:, 0:129],
                                    ptb[:, kl * 512 + sub * 128: kl * 512 + (sub + 1) * 128],
                                    Von[:, kt * 129:(kt + 1) * 129],
                                    start=(kp == 0 and kl == 0), stop=(kp == 31 and kl == 1)),
                                    inc=tt, waits=[(tv, mskd), (ld, vonr)])
                        sh.gset(("pt", kp % 2), tt)
                    sh.gset("kthfree", tt)
                    sh.gset("vonfree", tt)
                    sh.gset("qtcfree", tt)
                    avr = sh.now(tt)
                    for sub in range(4):
                        sh.op("vector", lambda e, pc=PCs[sub]: e.reciprocal(
                            Rq[:, :], pc[:, 128:129]), inc=tv, waits=[(tt, avr)])
                        sh.op("vector", lambda e, pc=PCs[sub], sub=sub, ch=ch, head=head:
                              e.tensor_scalar(
                                  ctx[:, (ch * 4 + sub) * D + head * 128:
                                      (ch * 4 + sub) * D + (head + 1) * 128],
                                  pc[:, 0:128], Rq[:, 0:1], None, ALU.mult), inc=tv)
                    sh.gset(("ps", 0), tv)
                    sh.gset(("ps", 1), tv)
                sh.gset("mskfree", tt)
            bar()

        with nc.psum_tensor("PO0", [128, 512], F32) as PO0, \
             nc.psum_tensor("PO1", [128, 512], F32) as PO1, \
             nc.psum_tensor("PTO", [128, 128], BF16) as PTO, \
             nc.psum_tensor("PTO2", [128, 128], BF16) as PTO2, \
             nc.sbuf_tensor("WoT", [128, DCH * D], BF16) as WoT, \
             nc.sbuf_tensor("ctxT", [128, DCH * 1024], BF16) as ctxT, \
             nc.sbuf_tensor("WldO", [128, D], F32) as WldO, \
             nc.sbuf_tensor("WldOB", [128, D], BF16) as WldOB, \
             nc.sbuf_tensor("AOut", [128, D], F32) as AOut:
            PTs = [PTO, PTO2]
            for i in range(DCH):
                sh.dma("sync", WldO[:, :], wo_h[i * 128:(i + 1) * 128, :], ld,
                       waits=sh.gwait("wldofree"))
                sh.op("vector", lambda e: e.tensor_copy(WldOB[:, :], WldO[:, :]),
                      inc=tv, waits=[(ld, sh.now(ld))] + sh.gwait("wldobfree"))
                sh.gset("wldofree", tv)
                cr = sh.now(tv)
                for j in range(DCH):
                    bank = PTs[j % 2]
                    sh.op("tensor", lambda e, b=bank, j=j: e.transpose(
                        b[:, :], WldOB[:, j * 128:(j + 1) * 128], IdB[:, :]),
                        inc=tt, waits=[(tv, cr)] + sh.gwait(("bk2", id(bank))))
                    sh.op("scalar", lambda e, b=bank, j=j, i=i: e.activation(
                        WoT[:, j * D + i * 128: j * D + (i + 1) * 128], b[:, :], AF.Copy),
                        inc=tsc, waits=[(tt, sh.now(tt))])
                    sh.gset(("bk2", id(bank)), tsc)
                sh.guard["wldobfree"] = (tt, sh.now(tt))
            for r in range(8):
                for j in range(DCH):
                    bank = PTs[(r * DCH + j) % 2]
                    sh.op("tensor", lambda e, b=bank, r=r, j=j: e.transpose(
                        b[:, :], ctx[:, r * D + j * 128: r * D + (j + 1) * 128], IdB[:, :]),
                        inc=tt, waits=sh.gwait(("bk2", id(bank))))
                    sh.op("scalar", lambda e, b=bank, r=r, j=j: e.activation(
                        ctxT[:, j * 1024 + r * 128: j * 1024 + (r + 1) * 128],
                        b[:, :], AF.Copy), inc=tsc, waits=[(tt, sh.now(tt))])
                    sh.gset(("bk2", id(bank)), tsc)
            ctr = sh.now(tsc)
            POs = [PO0, PO1]
            for r in range(8):
                for eb in range(4):
                    bank = POs[eb % 2]
                    for j in range(DCH):
                        w = [(tsc, ctr)]
                        if j == 0:
                            w += sh.gwait(("po", id(bank)))
                        sh.op("tensor", lambda e, b=bank, r=r, eb=eb, j=j: e.matmul(
                            b[:, :], ctxT[:, j * 1024 + r * 128: j * 1024 + (r + 1) * 128],
                            WoT[:, j * D + eb * 512: j * D + (eb + 1) * 512],
                            start=(j == 0), stop=(j == DCH - 1)), inc=tt, waits=w)
                    sh.op("vector", lambda e, b=bank, eb=eb: e.tensor_tensor(
                        AOut[:, eb * 512:(eb + 1) * 512], b[:, :],
                        Bob[:, eb * 512:(eb + 1) * 512], ALU.add),
                        inc=tv, waits=[(tt, sh.now(tt))] + sh.gwait("aoutfree"))
                    sh.gset(("po", id(bank)), tv)
                sh.dma("gpsimd", attn_o[r * 128:(r + 1) * 128, :], AOut[:, :], so,
                       waits=[(tv, sh.now(tv))])
                sh.guard["aoutfree"] = (so, sh.now(so))

        sh.op("sync", lambda e: None,
              waits=[(so, sh.now(so)), (ld, sh.now(ld)), (gl, sh.now(gl)),
                     (tsc, sh.now(tsc)), (tv, sh.now(tv)), (tt, sh.now(tt))])
        sh.replay(nc, serial=STREAMS)
    return nc


def run_l2(inputs, r1):
    if "l2" not in _CACHE:
        _CACHE["l2"] = build_l2()
    nc = _CACHE["l2"]
    def full_T(key):
        return np.concatenate([np.asarray(r1[c][key]) for c in range(NCORE)], axis=1)
    KT = np.ascontiguousarray(full_T("kt_out"))
    QT = full_T("qt_out")
    V = np.ascontiguousarray(np.concatenate([np.asarray(r1[c]["v_out"]) for c in range(NCORE)], axis=0))
    WSall = np.ascontiguousarray(np.concatenate(
        [np.asarray(r1[c]["ws_out"]) for c in range(NCORE)], axis=1))
    in_maps = []
    kg = np.arange(64)[:, None, None] * 128 + np.arange(128)[None, :, None]
    for c in range(NCORE):
        chunks = [c, 15 - c]
        qt_own = np.ascontiguousarray(np.concatenate(
            [QT[:, g * 512:(g + 1) * 512] for g in chunks], axis=1))
        masks = np.empty((2, 128, 64 * 512), dtype=BF16NP)
        for ci, g in enumerate(chunks):
            qg = g * 512 + np.arange(512)[None, None, :]
            m = (kg <= qg)
            masks[ci] = m.transpose(1, 0, 2).reshape(128, 64 * 512).astype(BF16NP)
        in_maps.append({
            "qt_own": qt_own, "kt_full": KT, "v_full": V,
            "wo": inputs["L0_attn_Wo"], "bo": inputs["L0_attn_bo"].reshape(1, D),
            "ws_all": WSall, "masks": masks,
            "wg_sl": inputs["L0_ffn_w_gate"][c * (II // 8):(c + 1) * (II // 8), :],
            "wu_sl": inputs["L0_ffn_w_up"][c * (II // 8):(c + 1) * (II // 8), :],
            "wd_sl": inputs["L0_ffn_w_down"][c * (D // 8):(c + 1) * (D // 8), :],
        })
    res = run_bass_kernel_spmd(nc, in_maps, core_ids=list(range(NCORE)))
    return res.results



def build_l3():
    nc = bass.Bass("TRN2", target_bir_lowering=False)
    cst = _consts(nc)
    xnt_h = nc.dram_tensor("xnt_own", [128, DCH * TC], BF16, kind="ExternalInput")
    at_h = nc.dram_tensor("attn_own", [TC, D], F32, kind="ExternalInput")
    x_h = nc.dram_tensor("x_own", [TC, D], F32, kind="ExternalInput")
    wgt_h = nc.dram_tensor("wgt", [D, II], BF16, kind="ExternalInput")
    wut_h = nc.dram_tensor("wut", [D, II], BF16, kind="ExternalInput")
    wdt_h = nc.dram_tensor("wdt", [II, D], BF16, kind="ExternalInput")
    sv_h = nc.dram_tensor("svals", [1, 4], F32, kind="ExternalInput")
    out_h = nc.dram_tensor("out_own", [TC, D], F32, kind="ExternalOutput")
    ffnt_d = nc.dram_tensor("ffnt_scr", [D, TC], BF16, kind="Internal")
    if DEBUG_L3:
        dbg_xq = nc.dram_tensor("dbg_xq", [128, DCH * TC], BF16, kind="ExternalOutput")
        dbg_z = nc.dram_tensor("dbg_z", [128, DCH * TC], F32, kind="ExternalOutput")
        dbg_cs = nc.dram_tensor("dbg_cs", [128, 3 * TC], F32, kind="ExternalOutput")
        dbg_y0 = nc.dram_tensor("dbg_y0", [128, 2 * TC], BF16, kind="ExternalOutput")
        dbg_ff = nc.dram_tensor("dbg_ff", [D, TC], BF16, kind="ExternalOutput")

    with ExitStack() as stk:
        sb = lambda n, s_, d: stk.enter_context(nc.sbuf_tensor(n, s_, d))
        sem = lambda n: stk.enter_context(nc.semaphore(n))

        IdB = sb("IdB", [128, 128], BF16)
        IdF = sb("IdF", [128, 128], F32)
        HadB = sb("HadB", [128, 128], BF16)
        OnesR = sb("OnesR", [1, 128], F32)
        Sv = sb("Sv", [1, 4], F32)
        Sbc4 = sb("Sbc4", [128, 4], F32)
        Mcol = sb("Mcol", [128, 8], F32)
        Mclip = sb("Mclip", [128, 8], F32)
        Rec = sb("Rec", [128, 8], F32)
        CSc = sb("CSc", [128, 8], F32)
        INVc = sb("INVc", [128, 8], F32)
        Rw8 = sb("Rw8", [8, 128], F32)
        CRow = sb("CRow", [1, TC], F32)
        IRow = sb("IRow", [1, TC], F32)
        CS = sb("CS", [128, TC], F32)
        INV = sb("INV", [128, TC], F32)
        Tmp = sb("Tmp", [128, TC], F32)
        TmpB = sb("TmpB", [128, TC], BF16)
        Zero = sb("Zero", [128, 1], F32)
        nc.const_aps.register(F32, 0.0, Zero[:, :])
        XQ = sb("XQ", [128, DCH * TC], BF16)

        ld = sem("ld"); gl = sem("gl"); tv = sem("tv"); tsc = sem("tsc")
        tt = sem("tt"); so = sem("so")
        ALLSEMS = [ld, gl, tv, tsc, tt, so]
        STREAMS = {"vector": tv, "scalar": tsc}
        sh = Sched()

        def bar():
            sh.barrier(ENGS, ALLSEMS, STREAMS)

        sh.dma("sync", IdB[:, :], cst["ident_bf"][:, :], ld)
        sh.dma("sync", IdF[:, :], cst["ident_f32"][:, :], ld)
        sh.dma("sync", HadB[:, :], cst["had_bf"][:, :], ld)
        sh.dma("sync", OnesR[:, :], cst["ones_row"][:, :], ld)
        sh.dma("sync", Sv[:, :], sv_h[:, :], ld)
        sh.dma("sync", XQ[:, :], xnt_h[:, :], ld)
        sh.op("vector", lambda e: e.memset(Zero[:, :], 0.0), inc=tv)

        qctr = [0]

        def quant(Zget, nch, cc, zdt):
            qctr[0] += 1
            qn = qctr[0]
            """Emit absmax -> scales -> (CS, INV) broadcast tiles.
            Zget(j) -> [128, TC] view of chunk j (dtype zdt)."""
            Mx = CS
            sh.op("vector", lambda e, z=Zget(0): e.scalar_tensor_tensor(
                Mx[:, :], z, -1.0, z, ALU.mult, ALU.max), inc=tv)
            for j in range(1, nch):
                sh.op("vector", lambda e, z=Zget(j): e.scalar_tensor_tensor(
                    Tmp[:, :], z, -1.0, z, ALU.mult, ALU.max), inc=tv)
                sh.op("vector", lambda e: e.tensor_tensor(
                    Mx[:, :], Mx[:, :], Tmp[:, :], ALU.max), inc=tv)
            mxr = sh.now(tv)
            with nc.psum_tensor(f"PTF3_{qn}", [128, 128], F32) as PTF3, \
                 nc.psum_tensor(f"PTF4_{qn}", [128, 128], F32) as PTF4, \
                 nc.psum_tensor(f"PBC3_{qn}", [128, 512], F32) as PBC3:
                PTs = [PTF3, PTF4]
                for r in range(8):
                    bank = PTs[r % 2]
                    sh.op("tensor", lambda e, b=bank, r=r: e.transpose(
                        b[:, :], Mx[:, r * 128:(r + 1) * 128], IdF[:, :]),
                        inc=tt, waits=[(tv, mxr)] + sh.gwait(("qb", id(bank))))
                    sh.op("scalar", lambda e, b=bank, r=r: e.activation(
                        Tmp[:, r * 128:(r + 1) * 128], b[:, :], AF.Copy),
                        inc=tsc, waits=[(tt, sh.now(tt))])
                    sh.gset(("qb", id(bank)), tsc)
                for r in range(8):
                    t0 = Tmp[:, r * 128:(r + 1) * 128]
                    emit_fold(sh, "vector", tv,
                              (t0[:, 0:64], t0[:, 64:128]),
                              Tmp[:, r * 128:r * 128 + 64], 128, ALU.max,
                              Mcol[:, r:r + 1],
                              waits0=[(tsc, sh.now(tsc))])
                sh.op("vector", lambda e, cc=cc: e.tensor_scalar(
                    Mclip[:, :], Mcol[:, :], cc, 1e-5, ALU.mult, ALU.max), inc=tv)
                sh.op("vector", lambda e: e.reciprocal(Rec[:, :], Mclip[:, :]), inc=tv)
                sh.op("vector", lambda e, cc=cc: e.tensor_scalar(
                    CSc[:, :], Rec[:, :], 127.0 * cc, None, ALU.mult), inc=tv)
                sh.op("vector", lambda e: e.tensor_scalar(
                    INVc[:, :], Mclip[:, :], 1.0 / 127.0, None, ALU.mult), inc=tv)
                for (colsrc, row) in ((CSc, CRow), (INVc, IRow)):
                    sh.op("tensor", lambda e, c=colsrc: e.transpose(
                        PTF3[0:8, :], c[:, :], IdF[:, :]),
                        inc=tt, waits=[(tv, sh.now(tv))] + sh.gwait(("qb", id(PTF3))))
                    sh.op("scalar", lambda e: e.activation(
                        Rw8[:, :], PTF3[0:8, :], AF.Copy),
                        inc=tsc, waits=[(tt, sh.now(tt))] + sh.gwait("rw8free"))
                    sh.gset(("qb", id(PTF3)), tsc)
                    sh.dma("gpsimd", row[0:1, :], Rw8[:, :], gl,
                           waits=[(tsc, sh.now(tsc))])
                    sh.guard["rw8free"] = (gl, sh.now(gl))
                rowr = sh.now(gl)
                for (row, bc) in ((CRow, CS), (IRow, INV)):
                    for hh in range(2):
                        sh.op("tensor", lambda e, row=row, hh=hh: e.matmul(
                            PBC3[:, :],
                            OnesR[0:1, :], row[0:1, hh * 512:(hh + 1) * 512],
                            start=True, stop=True),
                            inc=tt, waits=[(gl, rowr)] + sh.gwait(("qb", id(PBC3))))
                        sh.op("scalar", lambda e, bc=bc, hh=hh: e.activation(
                            bc[:, hh * 512:(hh + 1) * 512], PBC3[:, :], AF.Copy),
                            inc=tsc, waits=[(tt, sh.now(tt))])
                        sh.gset(("qb", id(PBC3)), tsc)

        with nc.sbuf_tensor("Z", [128, DCH * TC], F32) as Z:
            with nc.psum_tensor("PF0", [128, 512], F32) as PF0, \
                 nc.psum_tensor("PF1", [128, 512], F32) as PF1:
                PFs = [PF0, PF1]
                for j in range(DCH):
                    for hh in range(2):
                        bank = PFs[(2 * j + hh) % 2]
                        sh.op("tensor", lambda e, b=bank, j=j, hh=hh: e.matmul(
                            b[:, :], HadB[:, :],
                            XQ[:, j * TC + hh * 512: j * TC + (hh + 1) * 512],
                            start=True, stop=True),
                            inc=tt, waits=[(ld, sh.now(ld))] + sh.gwait(("fb", id(bank))))
                        sh.op("scalar", lambda e, b=bank, j=j, hh=hh: e.activation(
                            Z[:, j * TC + hh * 512: j * TC + (hh + 1) * 512],
                            b[:, :], AF.Copy), inc=tsc, waits=[(tt, sh.now(tt))])
                        sh.gset(("fb", id(bank)), tsc)
            h = 1
            while h < DCH:
                for base in range(0, DCH, 2 * h):
                    for off in range(h):
                        j, j2 = base + off, base + off + h
                        za = lambda j=j: Z[:, j * TC:(j + 1) * TC]
                        zb = lambda j2=j2: Z[:, j2 * TC:(j2 + 1) * TC]
                        w = [(tsc, sh.now(tsc))] if h == 1 else []
                        sh.op("vector", lambda e, j=j, j2=j2: e.tensor_tensor(
                            Tmp[:, :], Z[:, j * TC:(j + 1) * TC],
                            Z[:, j2 * TC:(j2 + 1) * TC], ALU.subtract), inc=tv, waits=w)
                        sh.op("vector", lambda e, j=j, j2=j2: e.tensor_tensor(
                            Z[:, j * TC:(j + 1) * TC], Z[:, j * TC:(j + 1) * TC],
                            Z[:, j2 * TC:(j2 + 1) * TC], ALU.add), inc=tv)
                        sh.op("vector", lambda e, j2=j2: e.tensor_copy(
                            Z[:, j2 * TC:(j2 + 1) * TC], Tmp[:, :]), inc=tv)
                h *= 2
            if DEBUG_L3:
                sh.dma("gpsimd", dbg_z[:, :], Z[:, :], so, waits=[(tv, sh.now(tv))])
            quant(lambda j: Z[:, j * TC:(j + 1) * TC], DCH, C1, F32)
            qr = sh.now(tsc)
            for j in range(DCH):
                sh.op("vector", lambda e, j=j: e.tensor_tensor(
                    TmpB[:, :].bitcast(I32) if False else Tmp[:, :].bitcast(I32),
                    Z[:, j * TC:(j + 1) * TC], CS[:, :], ALU.mult),
                    inc=tv, waits=[(tsc, qr)])
                sh.op("vector", lambda e: e.tensor_scalar(
                    Tmp[:, :], Tmp[:, :].bitcast(I32), -128.0, 127.0, ALU.max, ALU.min),
                    inc=tv)
                sh.op("vector", lambda e, j=j: e.tensor_tensor(
                    XQ[:, j * TC:(j + 1) * TC], Tmp[:, :], INV[:, :], ALU.mult), inc=tv)
            if DEBUG_L3:
                sh.dma("gpsimd", dbg_xq[:, :], XQ[:, :], so, waits=[(tv, sh.now(tv))])
                sh.dma("gpsimd", dbg_cs[:, 0:TC], CS[:, :], so)
                sh.dma("gpsimd", dbg_cs[:, TC:2 * TC], INV[:, :], so)
                sh.dma("gpsimd", dbg_cs[:, 2 * TC:2 * TC + 8], Mcol[:, :], so)
            bar()

        with nc.sbuf_tensor("Y", [128, ICH * TC], BF16) as Y:
            with nc.psum_tensor("PG", [128, 1024], F32) as PG, \
                 nc.psum_tensor("PU", [128, 1024], F32) as PU, \
                 nc.sbuf_tensor("WgS", [128, 2 * DCH * 128], BF16) as WgS, \
                 nc.sbuf_tensor("WuS", [128, 2 * DCH * 128], BF16) as WuS, \
                 nc.sbuf_tensor("TmG", [128, 1024], F32) as TmG:
                sh.op("tensor", lambda e: e.matmul(
                    PG[:, 0:4], OnesR[0:1, :], Sv[0:1, :], start=True, stop=True),
                    inc=tt, waits=[(ld, sh.now(ld))])
                sh.op("scalar", lambda e: e.activation(Sbc4[:, :], PG[:, 0:4], AF.Copy),
                      inc=tsc, waits=[(tt, sh.now(tt))])
                svr = sh.now(tsc)
                sh.guard["pgfree"] = (tsc, svr)
                for i in range(ICH):
                    buf = i % 2
                    sh.dma("sync",
                           WgS[:, buf * 2048:(buf + 1) * 2048].rearrange(
                               "p (j c) -> p j c", c=128),
                           wgt_h[:, i * 128:(i + 1) * 128].rearrange(
                               "(j p) c -> p j c", p=128), ld,
                           waits=sh.gwait(("wgf", buf)))
                    sh.dma("sync",
                           WuS[:, buf * 2048:(buf + 1) * 2048].rearrange(
                               "p (j c) -> p j c", c=128),
                           wut_h[:, i * 128:(i + 1) * 128].rearrange(
                               "(j p) c -> p j c", p=128), ld,
                           waits=sh.gwait(("wuf", buf)))
                    ldr = sh.now(ld)
                    for j in range(DCH):
                        for hh in range(2):
                            w = [(ld, ldr)]
                            if j == 0 and hh == 0:
                                w += sh.gwait("pgfree")
                            sh.op("tensor", lambda e, buf=buf, j=j, hh=hh: e.matmul(
                                PG[:, hh * 512:(hh + 1) * 512],
                                WgS[:, buf * 2048 + j * 128: buf * 2048 + (j + 1) * 128],
                                XQ[:, j * TC + hh * 512: j * TC + (hh + 1) * 512],
                                start=(j == 0), stop=(j == DCH - 1)), inc=tt, waits=w)
                    for j in range(DCH):
                        for hh in range(2):
                            sh.op("tensor", lambda e, buf=buf, j=j, hh=hh: e.matmul(
                                PU[:, hh * 512:(hh + 1) * 512],
                                WuS[:, buf * 2048 + j * 128: buf * 2048 + (j + 1) * 128],
                                XQ[:, j * TC + hh * 512: j * TC + (hh + 1) * 512],
                                start=(j == 0), stop=(j == DCH - 1)), inc=tt)
                    sh.gset(("wgf", buf), tt)
                    sh.gset(("wuf", buf), tt)
                    mmr = sh.now(tt)
                    sh.op("scalar", lambda e: e.activation(
                        TmG[:, :], PG[:, :], AF.Silu, scale=Sbc4[:, 0:1]),
                        inc=tsc, waits=[(tt, mmr), (tsc, svr)] + sh.gwait("tmgfree"))
                    sh.op("vector", lambda e, i=i: e.scalar_tensor_tensor(
                        Y[:, i * TC:(i + 1) * TC], PU[:, :], Sbc4[:, 1:2], TmG[:, :],
                        ALU.mult, ALU.mult), inc=tv, waits=[(tsc, sh.now(tsc))])
                    sh.gset("pgfree", tv)
                    sh.guard["tmgfree"] = (tv, sh.now(tv))
                if DEBUG_L3:
                    sh.dma("gpsimd", dbg_y0[:, :], Y[:, 0:2 * TC], so,
                           waits=[(tv, sh.now(tv))])
                bar()

            with nc.psum_tensor("PF0b", [128, 512], F32) as PF0b, \
                 nc.psum_tensor("PF1b", [128, 512], F32) as PF1b:
                PFs = [PF0b, PF1b]
                for i in range(ICH):
                    for hh in range(2):
                        bank = PFs[(2 * i + hh) % 2]
                        sh.op("tensor", lambda e, b=bank, i=i, hh=hh: e.matmul(
                            b[:, :], HadB[:, :],
                            Y[:, i * TC + hh * 512: i * TC + (hh + 1) * 512],
                            start=True, stop=True),
                            inc=tt, waits=[(tv, sh.now(tv))] + sh.gwait(("fb2", id(bank))))
                        sh.op("scalar", lambda e, b=bank, i=i, hh=hh: e.activation(
                            Y[:, i * TC + hh * 512: i * TC + (hh + 1) * 512],
                            b[:, :], AF.Copy), inc=tsc, waits=[(tt, sh.now(tt))])
                        sh.gset(("fb2", id(bank)), tsc)
            h = 1
            while h < ICH:
                for base in range(0, ICH, 2 * h):
                    for off in range(h):
                        j, j2 = base + off, base + off + h
                        w = [(tsc, sh.now(tsc))] if h == 1 else []
                        sh.op("vector", lambda e, j=j, j2=j2: e.tensor_tensor(
                            TmpB[:, :], Y[:, j * TC:(j + 1) * TC],
                            Y[:, j2 * TC:(j2 + 1) * TC], ALU.subtract), inc=tv, waits=w)
                        sh.op("vector", lambda e, j=j, j2=j2: e.tensor_tensor(
                            Y[:, j * TC:(j + 1) * TC], Y[:, j * TC:(j + 1) * TC],
                            Y[:, j2 * TC:(j2 + 1) * TC], ALU.add), inc=tv)
                        sh.op("vector", lambda e, j2=j2: e.tensor_copy(
                            Y[:, j2 * TC:(j2 + 1) * TC], TmpB[:, :]), inc=tv)
                h *= 2
            quant(lambda i: Y[:, i * TC:(i + 1) * TC], ICH, C2, BF16)
            qr = sh.now(tsc)
            for i in range(ICH):
                sh.op("vector", lambda e, i=i: e.tensor_copy(
                    Tmp[:, :], Y[:, i * TC:(i + 1) * TC]), inc=tv,
                    waits=[(tsc, qr)])
                sh.op("vector", lambda e: e.tensor_tensor(
                    Tmp[:, :].bitcast(I32), Tmp[:, :], CS[:, :], ALU.mult), inc=tv)
                sh.op("vector", lambda e: e.tensor_scalar(
                    Tmp[:, :], Tmp[:, :].bitcast(I32), -128.0, 127.0, ALU.max, ALU.min),
                    inc=tv)
                sh.op("vector", lambda e, i=i: e.tensor_tensor(
                    Y[:, i * TC:(i + 1) * TC], Tmp[:, :], INV[:, :], ALU.mult), inc=tv)
            bar()

            with nc.psum_tensor("PD", [128, 1024], F32) as PD, \
                 nc.psum_tensor("PD2", [128, 1024], F32) as PD2, \
                 nc.sbuf_tensor("WdS", [128, 2 * 32 * 128], BF16) as WdS, \
                 nc.sbuf_tensor("FTs", [128, TC], BF16) as FTs:
                PDs = [PD, PD2]
                for o in range(DCH):
                    pd = PDs[o % 2]
                    ldrs = []
                    for half in range(2):
                        sh.dma("sync",
                               WdS[:, half * 4096:(half + 1) * 4096].rearrange(
                                   "p (k c) -> p k c", c=128),
                               wdt_h[half * (II // 2):(half + 1) * (II // 2),
                                     o * 128:(o + 1) * 128].rearrange(
                                   "(k p) c -> p k c", p=128), ld,
                               waits=sh.gwait(("wdf", half)))
                        ldrs.append(sh.now(ld))
                    for half in range(2):
                        buf = half
                        ldr = ldrs[half]
                        for k in range(32):
                            kg = half * 32 + k
                            for hh in range(2):
                                w = [(ld, ldr)]
                                if kg == 0 and hh == 0:
                                    w += sh.gwait(("pdf", id(pd)))
                                sh.op("tensor", lambda e, pd=pd, buf=buf, k=k, kg=kg,
                                      hh=hh: e.matmul(
                                    pd[:, hh * 512:(hh + 1) * 512],
                                    WdS[:, buf * 4096 + k * 128: buf * 4096 + (k + 1) * 128],
                                    Y[:, kg * TC + hh * 512: kg * TC + (hh + 1) * 512],
                                    start=(kg == 0), stop=(kg == ICH - 1)),
                                    inc=tt, waits=w)
                        sh.gset(("wdf", buf), tt)
                    sh.op("scalar", lambda e, pd=pd: e.activation(
                        FTs[:, :], pd[:, :], AF.Identity, scale=Sbc4[:, 2:3], bias=0.0),
                        inc=tsc, waits=[(tt, sh.now(tt))] + sh.gwait("ftsfree"))
                    sh.gset(("pdf", id(pd)), tsc)
                    sh.dma("gpsimd", ffnt_d[o * 128:(o + 1) * 128, :], FTs[:, :], gl,
                           waits=[(tsc, sh.now(tsc))])
                    sh.guard["ftsfree"] = (gl, sh.now(gl))
                bar()

        if DEBUG_L3:
            with nc.sbuf_tensor("DbgF", [128, TC], BF16) as DbgF:
                for o in range(DCH):
                    sh.dma("sync", DbgF[:, :], ffnt_d[o * 128:(o + 1) * 128, :], ld,
                           waits=sh.gwait("dbgffree"))
                    sh.dma("gpsimd", dbg_ff[o * 128:(o + 1) * 128, :], DbgF[:, :], so,
                           waits=[(ld, sh.now(ld))])
                    sh.guard["dbgffree"] = (so, sh.now(so))
                bar()

        with nc.sbuf_tensor("FR", [128, D], BF16) as FR, \
             nc.sbuf_tensor("Xr", [128, D], F32) as Xr, \
             nc.sbuf_tensor("Ar", [128, D], F32) as Ar, \
             nc.sbuf_tensor("Ot", [128, D], F32) as Ot:
            for r in range(NT):
                sh.dma("sync", Xr[:, :], x_h[r * 128:(r + 1) * 128, :], ld,
                       waits=sh.gwait("xrfree"))
                sh.dma("sync", Ar[:, :], at_h[r * 128:(r + 1) * 128, :], ld)
                sh.op("sync", lambda e, r=r: e.dma_start(
                    out=FR[:, :], in_=ffnt_d[:, r * 128:(r + 1) * 128], transpose=True),
                    inc=ld, inc_n=16, waits=sh.gwait("frfree"))
                ldr = sh.now(ld)
                sh.op("vector", lambda e: e.tensor_tensor(
                    Ot[:, :], Xr[:, :], Ar[:, :], ALU.add), inc=tv,
                    waits=[(ld, ldr)] + sh.gwait("otfree"))
                sh.op("vector", lambda e: e.tensor_copy(Xr[:, :], FR[:, :]), inc=tv)
                sh.gset("frfree", tv)
                sh.op("vector", lambda e: e.tensor_tensor(
                    Ot[:, :], Ot[:, :], Xr[:, :], ALU.add), inc=tv)
                sh.gset("xrfree", tv)
                sh.dma("gpsimd", out_h[r * 128:(r + 1) * 128, :], Ot[:, :], so,
                       waits=[(tv, sh.now(tv))])
                sh.guard["otfree"] = (so, sh.now(so))

        sh.op("sync", lambda e: None,
              waits=[(so, sh.now(so)), (ld, sh.now(ld)), (gl, sh.now(gl)),
                     (tsc, sh.now(tsc)), (tv, sh.now(tv)), (tt, sh.now(tt))])
        sh.replay(nc, serial=STREAMS)
    return nc


def run_l3(inputs, r1, r2):
    if "l3" not in _CACHE:
        _CACHE["l3"] = build_l3()
    nc = _CACHE["l3"]
    x = np.ascontiguousarray(inputs["L0_attn_x"].reshape(S, D))
    attn_full = np.empty((S, D), np.float32)
    for c in range(NCORE):
        a = np.asarray(r2[c]["attn_out"])
        attn_full[c * 512:(c + 1) * 512] = a[0:512]
        attn_full[(15 - c) * 512:(16 - c) * 512] = a[512:1024]
    wgt = np.ascontiguousarray(np.concatenate(
        [np.asarray(r2[c]["wgt_out"]) for c in range(NCORE)], axis=1))
    wut = np.ascontiguousarray(np.concatenate(
        [np.asarray(r2[c]["wut_out"]) for c in range(NCORE)], axis=1))
    wdt = np.ascontiguousarray(np.concatenate(
        [np.asarray(r2[c]["wdt_out"]) for c in range(NCORE)], axis=1))
    svals = np.asarray(r2[0]["s_out"])
    in_maps = []
    for c in range(NCORE):
        in_maps.append({
            "xnt_own": np.asarray(r1[c]["xnt_out"]),
            "attn_own": attn_full[c * TC:(c + 1) * TC],
            "x_own": x[c * TC:(c + 1) * TC],
            "wgt": wgt, "wut": wut, "wdt": wdt, "svals": svals,
        })
    res = run_bass_kernel_spmd(nc, in_maps, core_ids=list(range(NCORE)))
    return res.results


def kernel(**inputs):
    r1 = run_l1(inputs)
    r2 = run_l2(inputs, r1)
    r3 = run_l3(inputs, r1, r2)
    out = np.concatenate([np.asarray(r3[c]["out_own"]) for c in range(NCORE)], axis=0)
    return out.reshape(1, S, D).astype(np.float32)

